# version 02
# baseline (speedup 1.0000x reference)
import numpy as np
import ml_dtypes
import concourse.bass as bass
import concourse.mybir as mybir

F32 = mybir.dt.float32
BF16 = mybir.dt.bfloat16
F16 = mybir.dt.float16
I32 = mybir.dt.int32
AF = mybir.ActivationFunctionType
OP = mybir.AluOpType
AP = bass.AP

HID = 4096; H = 128; P = 64; N = 128; G = 8; KC = 4; CHK = 128
INTER = 8192; CONV = 10240; PROJ = 18560
EPS = 1e-5
NCORES = 8; TOK = 1024; NCH = 8
NKT = 32; NOT1 = 81; TW = 1027; NHS = 64

ORIG_OF_PERM = (np.arange(H) // 16) + 8 * (np.arange(H) % 16)
INTER_PERM = (ORIG_OF_PERM[:, None] * P + np.arange(P)[None, :]).reshape(-1)

ACTIVE = [None]


class Q:
    def __init__(self, k, name):
        self.k = k; self.name = name; self.eng = None
        self.counts = {}

    def _wait(self, lb):
        k = self.k
        if k.dry:
            assert lb in k.labels or True
            return
        s, v = k.labels[lb]
        if v > 0:
            self.eng.wait_ge(k.sems[s], v)

    SELFSEM = {'vector': 'v', 'scalar': 'sc'}

    def op(self, f, sem=None, inc=1, label=None, waits=()):
        k = self.k
        emit = (not k.dry) and (ACTIVE[0] == self.name)
        if emit:
            ss = self.SELFSEM.get(self.name)
            if ss is not None:
                c = self.counts.get(ss, 0)
                if c > 0:
                    self.eng.wait_ge(k.sems[ss], c)
            for lb in waits:
                self._wait(lb)
            ins = f(self.eng)
            if sem is not None:
                ins.then_inc(k.sems[sem], inc)
        if sem is not None:
            self.counts[sem] = self.counts.get(sem, 0) + inc
            if label is not None and k.dry:
                k.labels[label] = (sem, self.counts[sem])

    def wait(self, lb):
        if (not self.k.dry) and ACTIVE[0] == self.name:
            self._wait(lb)


class Sched:
    def __init__(self):
        self.labels = {}; self.dry = True; self.sems = {}


def prep_inputs(ins):
    x = np.ascontiguousarray(np.asarray(ins['L0_attn_hidden_states'])[0])
    Win = np.asarray(ins['L0_attn_in_proj_w'])
    Wg = Win[0:INTER][INTER_PERM]
    Whs = Win[INTER:2*INTER][INTER_PERM]
    WB = Win[2*INTER:2*INTER+G*N]
    WC = Win[2*INTER+G*N:INTER+CONV]
    Wdt = Win[INTER+CONV:][ORIG_OF_PERM]
    W1 = np.concatenate([Wdt, WB, WC, Whs], 0)
    w1t = np.ascontiguousarray(
        W1.reshape(NOT1, 128, NKT, 128).transpose(0, 2, 3, 1)).reshape(NOT1*NKT*128, 128)
    wgt = np.ascontiguousarray(
        Wg.reshape(16, 512, NKT, 128).transpose(0, 2, 3, 1)).reshape(16*NKT*128, 512)
    Wout = np.asarray(ins['L0_attn_out_proj_w'])[:, INTER_PERM]
    wot = np.ascontiguousarray(
        Wout.reshape(8, 512, 64, 128).transpose(0, 2, 3, 1)).reshape(8*64*128, 512)
    Wp = np.asarray(ins['L0_ffn_weight_packed'])
    WpT = np.ascontiguousarray(Wp.T)
    pkd = np.ascontiguousarray(
        WpT.reshape(16, 128, 8, 512).transpose(2, 0, 1, 3)).reshape(8*16*128, 512)
    Sc = np.asarray(ins['L0_ffn_scales'])
    ScT = np.ascontiguousarray(Sc.T)
    kt2 = np.arange(32)[:, None]
    pp = np.arange(128)[None, :]
    bmap = (2*((kt2 % 16)*128 + pp) + kt2 // 16) // 32
    scr = np.ascontiguousarray(
        ScT.reshape(128, 8, 512)[bmap].transpose(2, 0, 1, 3)).reshape(8*32*128, 512)
    cw = np.asarray(ins['L0_attn_conv_w'])[:, 0, :]
    cb = np.asarray(ins['L0_attn_conv_b'])
    cw_r = np.concatenate([cw[INTER:INTER+G*N], cw[INTER+G*N:], cw[0:INTER][INTER_PERM]], 0)
    cb_r = np.concatenate([cb[INTER:INTER+G*N], cb[INTER+G*N:], cb[0:INTER][INTER_PERM]], 0)
    cw4 = np.ascontiguousarray(cw_r.reshape(80, 128, 4).transpose(1, 0, 2)).reshape(128, 320)
    cb1 = np.ascontiguousarray(cb_r.reshape(80, 128).T)
    dtb = np.ascontiguousarray(np.asarray(ins['L0_attn_dt_bias'])[ORIG_OF_PERM][:, None])
    alog = np.ascontiguousarray(np.asarray(ins['L0_attn_A_log'])[ORIG_OF_PERM][:, None])
    dvec = np.ascontiguousarray(
        np.broadcast_to(np.asarray(ins['L0_attn_D'])[ORIG_OF_PERM][None, :], (128, 128)))
    nwv = np.asarray(ins['L0_attn_norm_weight'])[INTER_PERM]
    nwsb = np.ascontiguousarray(nwv.reshape(64, 128).T)
    xT = np.ascontiguousarray(x.T)
    maps = []
    for c in range(NCORES):
        t0 = c * TOK
        halo = np.zeros((HID, 3), np.float32) if t0 < 3 else np.ascontiguousarray(xT[:, t0-3:t0])
        xT_ch = np.ascontiguousarray(np.concatenate([halo, xT[:, t0:t0+TOK]], 1))
        xrows = np.ascontiguousarray(x[t0:t0+TOK])
        msel = np.zeros((8, 8), np.float32)
        for j in range(c):
            msel[j, j+1:c] = 1
        mmask = np.zeros((8, 1), np.float32); mmask[:c, 0] = 1
        maps.append(dict(xT_ch=xT_ch, xrows=xrows, w1t=w1t, wgt=wgt, wot=wot,
                         pkd=pkd, scr=scr, cw4=cw4, cb1=cb1, dtb=dtb, alog=alog,
                         dvec=dvec, nwsb=nwsb,
                         mselT=np.ascontiguousarray(msel.T), mmask=mmask))
    return maps


def build(dbg=(), nphase=99):
    nc = bass.Bass('TRN2', target_bir_lowering=False)
    D = {}
    for nm, shp, dt in [
            ('xT_ch', [HID, TW], F32), ('xrows', [TOK, HID], F32),
            ('w1t', [NOT1*NKT*128, 128], F32), ('wgt', [16*NKT*128, 512], F32),
            ('wot', [8*64*128, 512], F32), ('pkd', [8*16*128, 512], I32),
            ('scr', [8*32*128, 512], F16), ('cw4', [128, 320], F32),
            ('cb1', [128, 80], F32), ('dtb', [128, 1], F32), ('alog', [128, 1], F32),
            ('dvec', [128, 128], F32), ('nwsb', [128, 64], F32),
            ('mselT', [8, 8], F32), ('mmask', [8, 1], F32)]:
        D[nm] = nc.dram_tensor(nm, shp, dt, kind='ExternalInput')
    out = nc.dram_tensor('out', [TOK, HID], F32, kind='ExternalOutput')

    hsT_scr = nc.dram_tensor('hsT_scr', [NHS*NCH*128, 128], BF16)
    yloc_scr = nc.dram_tensor('yloc_scr', [NCH*128, INTER], BF16)
    yfull_scr = nc.dram_tensor('yfull_scr', [NCH*128, INTER], BF16)
    gate_scr = nc.dram_tensor('gate_scr', [NCH*128, 16*512], BF16)
    x2_scr = nc.dram_tensor('x2_scr', [NCH*8*128, 512], BF16)
    x2n_scr = nc.dram_tensor('x2n_scr', [NCH*128, HID], F32)
    x3_scr = nc.dram_tensor('x3_scr', [NCH*8*128, 512], BF16)
    aginR = nc.dram_tensor('aginR', [128, INTER], F32)
    agoutR = nc.dram_tensor('agoutR', [NCORES*128, INTER], F32)
    aginA = nc.dram_tensor('aginA', [1, 128], F32)
    agoutA = nc.dram_tensor('agoutA', [NCORES, 128], F32)
    DBG = {}
    dbg_specs = {'d_hsT': ([NHS*NCH*128, 128], BF16), 'd_bfm': ([128, 8192], BF16),
                 'd_cfm': ([128, 8192], BF16), 'd_dt': ([128, 1024], F32),
                 'd_acs': ([128, 1024], F32), 'd_yloc': ([NCH*128, INTER], BF16),
                 'd_T': ([128, INTER], F32), 'd_init': ([128, INTER], F32),
                 'd_yfull': ([NCH*128, INTER], BF16), 'd_gate': ([NCH*128, 16*512], BF16),
                 'd_x2n': ([NCH*128, HID], F32), 'd_x2': ([NCH*8*128, 512], BF16),
                 'd_normt': ([128, NCH*NHS*128], BF16)}
    for nm in dbg:
        shp, dt = dbg_specs[nm]
        DBG[nm] = nc.dram_tensor(nm, shp, dt, kind='ExternalOutput')

    ident_bf = nc.inline_tensor(np.eye(128, dtype=np.float32).astype(ml_dtypes.bfloat16),
                                'ident_bf')
    ident_f = nc.inline_tensor(np.eye(128, dtype=np.float32), 'ident_f')
    um_np = (np.arange(128)[:, None] <= np.arange(128)[None, :]).astype(np.float32)
    umask_bf = nc.inline_tensor(um_np.astype(ml_dtypes.bfloat16), 'umask_bf')
    ones_row = nc.inline_tensor(np.ones((1, 128), np.float32), 'ones_row')
    shv_t = nc.inline_tensor(((np.arange(128) % 2) * 4).astype(np.int32)[:, None], 'shv')

    CBASE = 20608
    LBASE = 24704
    PHB = 102656
    SBTOP = 229344
    SBOFF = {}
    def sal(name, shape, dtype, off):
        per = int(np.prod(shape[1:])) * mybir.dt.size(dtype)
        assert CBASE <= off and off + per <= SBTOP, (name, off, per)
        SBOFF[name] = (off, off + per)
        return nc.alloc_sbuf_tensor_at(name, shape, dtype, offset=off)

    co = [CBASE]
    def cal(name, shape, dtype):
        t = sal(name, shape, dtype, co[0])
        co[0] += (SBOFF[name][1] - SBOFF[name][0] + 31) & ~31
        return t
    idb = cal('idb', [128, 128], BF16)
    idf = cal('idf', [128, 128], F32)
    umb = cal('umb', [128, 128], BF16)
    onesr = cal('onesr', [1, 128], F32)
    cw_s = cal('cw_s', [128, 320], F32)
    cb_s = cal('cb_s', [128, 80], F32)
    dtb_s = cal('dtb_s', [128, 1], F32)
    acol = cal('acol', [128, 1], F32)
    drow = cal('drow', [128, 128], F32)
    nw_s = cal('nw_s', [128, 64], F32)
    msel_s = cal('msel_s', [8, 8], F32)
    mmask_s = cal('mmask_s', [8, 1], F32)
    shv_s = cal('shv_s', [128, 1], I32)
    assert co[0] <= 24704, co
    lo = [LBASE]
    def lal(name, shape, dtype):
        t = sal(name, shape, dtype, lo[0])
        lo[0] += (SBOFF[name][1] - SBOFF[name][0] + 31) & ~31
        return t
    bfm = lal('bfm', [128, 8*1024], BF16)
    cfm = lal('cfm', [128, 8*1024], BF16)
    dtf = lal('dtf', [128, 1024], F32)
    acsf = lal('acsf', [128, 1024], F32)
    dexp = lal('dexp', [128, 8*128], F32)
    asum_s = lal('asum_s', [128, 4], F32)
    rloc = lal('rloc', [128, INTER], F32)
    assert lo[0] <= PHB, lo
    PH = PHB
    PHTOP = SBTOP

    ps = nc.alloc_psum_tensor('ps8', [128, 4096], F32)
    def bank(b, w=512, o=0):
        return ps[:, b*512+o:b*512+o+w]
    def bankbf(b, o=0, w=128):
        return AP(ps, b*512, [[4096, 128], [1, 512]]).bitcast(BF16)[:, o:o+w]

    K = Sched()
    for s in ['xld', 'wld0', 'wld1', 'mm', 'mm2', 'tr', 'v', 'sc', 'g',
              'st0', 'st1', 'ld0', 'ld1', 'cst', 'cc', 'gd0', 'gd1', 'dbg', 'st2', 'gd2']:
        K.sems[s] = nc.alloc_semaphore(name=s)
    qs = Q(K, 'sync'); qg = Q(K, 'gpsimd'); qt = Q(K, 'tensor')
    qv = Q(K, 'vector'); qc = Q(K, 'scalar')
    ALLQ = {'sync': qs, 'gpsimd': qg, 'tensor': qt, 'vector': qv, 'scalar': qc}

    def phase_A():
        consts = [(idb, ident_bf), (idf, ident_f), (umb, umask_bf), (onesr, ones_row),
                  (cw_s, D['cw4']), (cb_s, D['cb1']), (dtb_s, D['dtb']), (acol, D['alog']),
                  (drow, D['dvec']), (nw_s, D['nwsb']), (msel_s, D['mselT']),
                  (mmask_s, D['mmask']), (shv_s, shv_t)]
        for i, (dst, src) in enumerate(consts):
            qs.op(lambda e, d_=dst, s_=src: e.dma_start(d_[:, :], s_[:, :]),
                  sem='cst', inc=16, label=('A.c', i))
        qc.op(lambda e: e.activation(acol[:, :], acol[:, :], AF.Exp),
              sem='sc', label='A.exp', waits=[('A.c', len(consts) - 1)])
        qv.op(lambda e: e.tensor_scalar_mul(acol[:, :], acol[:, :], -1.0),
              sem='v', label='A.neg', waits=['A.exp'])
        qv.op(lambda e: e.memset(rloc[:, :], 0.0), sem='v', label='A.rz')
        qv.op(lambda e: e.memset(asum_s[:, :], 0.0), sem='v', label='A.az')

    xTb = sal('xTb', [128, NKT*TW], BF16, PH)
    oB = PH + ((NKT*TW*2 + 31) & ~31)
    def _al(x):
        return (x + 31) & ~31
    wslab = sal('wslab', [128, 2*NKT*128], BF16, oB); oB = _al(oB + 2*NKT*128*2)
    rawb = sal('rawb', [128, 2*1028], BF16, oB); oB = _al(oB + 2*1028*2)
    accb = sal('accb', [128, 2*1024], F32, oB); oB = _al(oB + 2*1024*4)
    actb = sal('actb', [128, 2*1024], BF16, oB); oB = _al(oB + 2*1024*2)
    stgb = sal('stgb', [128, 2*1024], BF16, oB); oB = _al(oB + 2*1024*2)
    dtrb = sal('dtrb', [128, 1024], F32, oB); oB = _al(oB + 1024*4)
    tmpb = sal('tmpb', [128, 1024], F32, oB); oB = _al(oB + 1024*4)
    assert oB <= PHTOP

    def phase_B():
        qg.op(lambda e: e.dma_start(
            AP(xTb, 0, [[NKT*TW, 128], [TW, NKT], [1, TW]]),
            AP(D['xT_ch'], 0, [[TW, 128], [128*TW, NKT], [1, TW]])),
            sem='xld', inc=16, label='B.x')
        for ot in range(NOT1):
            par = ot % 2
            w = [('B.mm', ot - 2)] if ot >= 2 else []
            qg.op(lambda e, o_=ot, p_=par: e.dma_start(
                AP(wslab, p_*NKT*128, [[2*NKT*128, 128], [128, NKT], [1, 128]]),
                AP(D['w1t'], o_*NKT*128*128, [[128, 128], [128*128, NKT], [1, 128]])),
                sem='wld%d' % par, inc=16, label=('B.w', ot), waits=w)
        qt.wait('B.x')
        def hs_transposes(ot):
            hot = ot - 18
            tw = [('B.act', ot - 1)]
            if hot >= 2:
                tw.append(('B.st', hot - 2))
            for j in range(NCH):
                qt.op(lambda e, j_=j, o_=ot: e.transpose(
                    bankbf(6 + (o_ - 1) % 2, j_*128),
                    AP(actb, ((o_-1) % 2)*1024 + j_*128, [[2*1024, 128], [1, 128]]),
                    idb[:, :]),
                    sem='tr', inc=1, label=('B.tr', hot, j), waits=tw if j == 0 else ())
        for ot in range(NOT1):
            par = ot % 2
            w0 = [('B.w', ot)] + ([('B.ev', ot - 2)] if ot >= 2 else [])
            for ti, (c0, wd) in enumerate([(0, 512), (512, 512), (1024, 3)]):
                pt = bank(3*par + ti, wd)
                for k in range(NKT):
                    last = (ti == 2 and k == NKT - 1)
                    qt.op(lambda e, k_=k, p_=pt, c_=c0, w_=wd, q_=par: e.matmul(
                        p_, AP(wslab, q_*NKT*128 + k_*128, [[2*NKT*128, 128], [1, 128]]),
                        AP(xTb, k_*TW + c_, [[NKT*TW, 128], [1, w_]]),
                        start=(k_ == 0), stop=(k_ == NKT - 1)),
                        sem='mm' if last else None, inc=1,
                        label=('B.mm', ot) if last else None,
                        waits=w0 if (ti == 0 and k == 0) else ())
            if 0 <= ot - 18 < NHS:
                hs_transposes(ot)
        hs_transposes(NOT1)
        for ot in range(NOT1 + 2):
            par = ot % 2
            if ot < NOT1:
                rb = AP(rawb, par*1028, [[2*1028, 128], [1, 1027]])
                pt0 = bank(3*par); pt1 = bank(3*par + 1); pt2 = bank(3*par + 2, 3)
                wv = [('B.mm', ot)]
                if ot == 2:
                    wv.append('B.dtraw')
                elif ot >= 3:
                    wv.append(('B.conv', ot - 3))
                qv.op(lambda e, r_=rb, p_=pt0: e.tensor_copy(r_[:, 0:512], p_), sem='v',
                      waits=wv)
                qv.op(lambda e, r_=rb, p_=pt1: e.tensor_copy(r_[:, 512:1024], p_), sem='v')
                qv.op(lambda e, r_=rb, p_=pt2: e.tensor_copy(r_[:, 1024:1027], p_), sem='v',
                      label=('B.ev', ot))
                if ot == 0:
                    qv.op(lambda e, r_=rb: e.tensor_copy(dtrb[:, :], r_[:, 3:1027]),
                          sem='v', label='B.dtraw')
                else:
                    ci = ot - 1
                    acc = AP(accb, par*1024, [[2*1024, 128], [1, 1024]])
                    wc = [('B.act', ci - 1)] if ci >= 2 else []
                    qv.op(lambda e, r_=rb, a_=acc, c_=ci: e.tensor_scalar_mul(
                        a_, r_[:, 0:1024], cw_s[:, 4*c_:4*c_+1]), sem='v', waits=wc)
                    for k in range(1, 4):
                        qv.op(lambda e, r_=rb, a_=acc, c_=ci, k_=k: e.scalar_tensor_tensor(
                            a_, r_[:, k_:k_+1024], cw_s[:, 4*c_+k_:4*c_+k_+1], a_,
                            OP.mult, OP.add), sem='v',
                            label=('B.conv', ci) if k == 3 else None)
            hot2 = ot - 19
            if 0 <= hot2 < NHS:
                for j in range(NCH):
                    wj = []
                    if j == 0:
                        wj = [('B.tr', hot2, 7)]
                        if hot2 >= 2:
                            wj.append(('B.st', hot2 - 2))
                    qv.op(lambda e, h_=hot2, j_=j: e.tensor_copy(
                        AP(stgb, (h_ % 2)*1024 + j_*128, [[2*1024, 128], [1, 128]]),
                        bankbf(6 + (h_ + 17) % 2, j_*128)), sem='v',
                        waits=wj, label=('B.stg', hot2) if j == 7 else None)
        for ot in range(1, NOT1):
            par = ot % 2; ci = ot - 1
            acc = AP(accb, par*1024, [[2*1024, 128], [1, 1024]])
            if ci < 8:
                dst = AP(bfm, ci*1024, [[8*1024, 128], [1, 1024]])
            elif ci < 16:
                dst = AP(cfm, (ci-8)*1024, [[8*1024, 128], [1, 1024]])
            else:
                dst = AP(actb, par*1024, [[2*1024, 128], [1, 1024]])
            w = [('B.conv', ci)]
            if ci >= 18:
                w.append(('B.tr', ci - 18, 7))
            qc.op(lambda e, d_=dst, a_=acc, c_=ci: e.activation(
                d_, a_, AF.Silu, bias=cb_s[:, c_:c_+1]), sem='sc', label=('B.act', ci + 1),
                waits=w)
        qc.op(lambda e: e.activation(dtrb[:, :], dtrb[:, :], AF.Identity, bias=dtb_s[:, :]),
              sem='sc', label='B.z', waits=['B.dtraw'])
        qc.op(lambda e: e.activation(tmpb[:, :], dtrb[:, :], AF.Abs), sem='sc', label='B.abs',
              waits=['B.z'])
        qc.op(lambda e: e.activation(tmpb[:, :], tmpb[:, :], AF.Exp, scale=-1.0), sem='sc',
              label='B.e', waits=['B.abs'])
        qc.op(lambda e: e.activation(tmpb[:, :], tmpb[:, :], AF.Ln, bias=1.0), sem='sc',
              label='B.l', waits=['B.e'])
        qv.op(lambda e: e.tensor_scalar_max(dtf[:, :], dtrb[:, :], 0.0), sem='v',
              label='B.r', waits=['B.z'])
        qv.op(lambda e: e.tensor_add(dtf[:, :], dtf[:, :], tmpb[:, :]), sem='v',
              label='B.dt', waits=['B.l', 'B.r'])
        qv.op(lambda e: e.tensor_scalar_mul(tmpb[:, :], dtf[:, :], acol[:, :]), sem='v',
              label='B.adt', waits=['A.neg', 'B.l'])
        for j in range(NCH):
            qv.op(lambda e, j_=j: e.tensor_tensor_scan(
                acsf[:, j_*128:(j_+1)*128], tmpb[:, j_*128:(j_+1)*128],
                tmpb[:, j_*128:(j_+1)*128], 0.0, OP.add, OP.bypass), sem='v',
                label=('B.acs', j), waits=['B.adt'] if j == 0 else [])
        for j in range(NCH):
            qv.op(lambda e, j_=j: e.tensor_add(
                asum_s[:, 0:1], asum_s[:, 0:1], acsf[:, j_*128+127:j_*128+128]), sem='v',
                label='B.asum' if j == 7 else None,
                waits=['A.az', ('B.acs', 7)] if j == 0 else [])
        for hot in range(NHS):
            qs.op(lambda e, h_=hot: e.dma_start(
                AP(hsT_scr, h_*NCH*128*128, [[128, 128], [128*128, NCH], [1, 128]]),
                AP(stgb, (h_ % 2)*1024, [[2*1024, 128], [128, NCH], [1, 128]])),
                sem='st%d' % (hot % 2), inc=16, label=('B.st', hot), waits=[('B.stg', hot)])

    def phase_Bdbg():
        if 'd_hsT' in DBG:
            qs.op(lambda e: e.dma_start(DBG['d_hsT'][:, :], hsT_scr[:, :]), sem='dbg',
                  inc=16, label='dbg.d_hsT', waits=[('B.st', 62), ('B.st', 63)])
        taps = [('d_bfm', lambda: bfm[:, :]), ('d_cfm', lambda: cfm[:, :]),
                ('d_dt', lambda: dtf[:, :]), ('d_acs', lambda: acsf[:, :])]
        w = [('B.act', 16), 'B.dt', ('B.acs', 7)]
        first = True
        for nm, apf in taps:
            if nm in DBG:
                qs.op(lambda e, n_=nm, a_=apf: e.dma_start(DBG[n_][:, :], a_()),
                      sem='dbg', inc=16, label='dbg.' + nm, waits=w if first else [])
                first = False


    oC = PHB
    def _alc(x):
        return (x + 31) & ~31
    hstb = sal('hstb', [128, 2*8192], BF16, oC); oC = _alc(oC + 2*8192*2)
    xj = sal('xj', [128, 8192], BF16, oC); oC = _alc(oC + 8192*2)
    dhs = sal('dhs', [128, 8192], BF16, oC); oC = _alc(oC + 8192*2)
    ylb = sal('ylb', [128, 8192], BF16, oC); oC = _alc(oC + 8192*2)
    prb = sal('prb', [128, 8192], BF16, oC); oC = _alc(oC + 8192*2)
    mtb = sal('mtb', [128, 2*1024], BF16, oC); oC = _alc(oC + 2*1024*2)
    relub = sal('relub', [128, 1024], F32, oC); oC = _alc(oC + 1024*4)
    gmub = sal('gmub', [128, 1024], BF16, oC); oC = _alc(oC + 1024*2)
    xdg = sal('xdg', [128, 2*1024], BF16, oC); oC = _alc(oC + 2*1024*2)
    btb = sal('btb', [128, 1024], BF16, oC); oC = _alc(oC + 1024*2)
    yevt = sal('yevt', [128, 512], BF16, oC); oC = _alc(oC + 512*2)
    acstm = sal('acstm', [128, 128], F32, oC); oC = _alc(oC + 128*4)
    dtmb = sal('dtmb', [128, 128], BF16, oC); oC = _alc(oC + 128*2)
    dcyb = sal('dcyb', [128, 128], BF16, oC); oC = _alc(oC + 128*2)
    sdob = sal('sdob', [128, 128], BF16, oC); oC = _alc(oC + 128*2)
    dfmt = sal('dfmt', [128, 128], BF16, oC); oC = _alc(oC + 128*2)
    d1b = sal('d1b', [1, 128], F32, oC); oC = _alc(oC + 128*4)
    d1e = sal('d1e', [1, 128], F32, oC); oC = _alc(oC + 128*4)
    flatb = sal('flatb', [1, 2048], F32, oC); oC = _alc(oC + 2048*4)
    assert oC <= SBTOP, oC

    def phase_C():
        GSL = 1024
        for j in range(NCH):
            par = j % 2
            wld_ = [('B.st', 62), ('B.st', 63)] if j == 0 else \
                   ([('C.dhs', j - 2)] if j >= 2 else [])
            qs.op(lambda e, j_=j, p_=par: e.dma_start(
                AP(hstb, p_*8192, [[2*8192, 128], [128, 64], [1, 128]]),
                AP(hsT_scr, j_*128*128, [[128, 128], [8*128*128, 64], [1, 128]])),
                sem='ld%d' % par, inc=16, label=('C.hs', j), waits=wld_)
            qs.op(lambda e, j_=j: e.dma_start(
                AP(yloc_scr, j_*128*INTER, [[INTER, 128], [1, INTER]]), ylb[:, :]),
                sem='st%d' % par, inc=16, label=('C.yst', j), waits=[('C.yev', j, 15)])
            wd1 = [('B.acs', 7)] + ([('C.d1e', j - 1)] if j >= 1 else [])
            qg.op(lambda e, j_=j: e.dma_start(
                AP(d1b, 0, [[128, 1], [1, 128]]),
                AP(acsf, j_*128 + 127, [[1024, 128], [1, 1]])),
                sem='gd0', inc=16, label=('C.d1', j), waits=wd1)
            for g in range(G):
                wfl = [('B.acs', 7)] + ([('C.acl', j, 2*g - 1)] if g >= 1 else
                                        ([('C.acl', j - 1, 15)] if j >= 1 else []))
                qg.op(lambda e, j_=j, g_=g: e.dma_start(
                    AP(flatb, 0, [[2048, 1], [128, 16], [1, 128]]),
                    AP(acsf, (g_*16)*1024 + j_*128, [[1024, 16], [1, 128]])),
                    sem='gd1', inc=16, label=('C.fl', j, g), waits=wfl)
            wt0 = [('B.acs', 7), 'B.dt']
            if j == 0:
                wt0 += [('B.stg', 62), ('B.stg', 63), ('B.ev', 79), ('B.ev', 80)]
            else:
                wt0 += [('C.yev', j - 1, 15), ('C.dexq', j - 1), ('C.rel', j - 1, 15)]
            qt.op(lambda e, j_=j: e.transpose(
                bank(0, 128), dtf[:, j_*128:(j_+1)*128], idf[:, :]),
                sem='tr', inc=1, label=('C.tdt', j), waits=wt0)
            qt.op(lambda e, j_=j: e.transpose(
                bank(1, 128), acsf[:, j_*128:(j_+1)*128], idf[:, :]),
                sem='tr', inc=1, label=('C.tacs', j))
            qv.op(lambda e: e.tensor_copy(dtmb[:, :], bank(0, 128)), sem='v',
                  label=('C.dtm', j), waits=[('C.tdt', j)])
            wacst = [('C.tacs', j)] + ([('C.rel', j - 1, 15)] if j >= 1 else [])
            qv.op(lambda e: e.tensor_copy(acstm[:, :], bank(1, 128)), sem='v',
                  label=('C.acst', j), waits=wacst)
            wsdo = [('C.acst', j)] + ([('C.yev', j - 1, 15)] if j >= 1 else [])
            qc.op(lambda e: e.activation(sdob[:, :], acstm[:, :], AF.Exp), sem='sc',
                  label=('C.sdo', j), waits=wsdo)
            wdfm = [('B.acs', 7)] + ([('C.tdcy', j - 1)] if j >= 1 else [])
            qc.op(lambda e, j_=j: e.activation(
                dfmt[:, :], acsf[:, j_*128:(j_+1)*128], AF.Exp, scale=-1.0,
                bias=acsf[:, j_*128+127:j_*128+128]), sem='sc',
                label=('C.dfm', j), waits=wdfm)
            qt.op(lambda e: e.transpose(bankbf(3, 0), dfmt[:, :], idb[:, :]),
                  sem='tr', inc=1, label=('C.tdcy', j), waits=[('C.dfm', j)])
            qv.op(lambda e: e.tensor_copy(dcyb[:, :], bankbf(3, 0)), sem='v',
                  label=('C.dcy', j), waits=[('C.tdcy', j)])
            wd1e = [('C.d1', j)] + ([('C.dexm', j - 1)] if j >= 1 else [])
            qc.op(lambda e: e.activation(d1e[0:1, :], d1b[0:1, :], AF.Exp), sem='sc',
                  label=('C.d1e', j), waits=wd1e)
            qt.op(lambda e: e.matmul(bank(2, 128), onesr[0:1, :], d1e[0:1, :]),
                  sem='mm2', inc=1, label=('C.dexm', j), waits=[('C.d1e', j)])
            qv.op(lambda e, j_=j: e.tensor_copy(dexp[:, j_*128:(j_+1)*128], bank(2, 128)),
                  sem='v', label=('C.dexq', j), waits=[('C.dexm', j)])
            wx = [('C.hs', j), ('C.dtm', j)] + ([('C.ymm', j - 1, 15)] if j >= 1 else [])
            qv.op(lambda e, p_=par: e.tensor_mul(
                xj[:, :], AP(hstb, p_*8192, [[2*8192, 128], [1, 8192]]),
                AP(dtmb, 0, [[128, 128], [1, 128], [0, 64]])), sem='v',
                label=('C.X', j), waits=wx)
            wd = [('A.c', 12)] + ([('C.yev', j - 1, 15)] if j >= 1 else [])
            qv.op(lambda e, p_=par: e.tensor_mul(
                dhs[:, :], AP(hstb, p_*8192, [[2*8192, 128], [1, 8192]]),
                AP(drow, 0, [[128, 128], [1, 128], [0, 64]])), sem='v',
                label=('C.dhs', j), waits=wd)
            for g in range(G):
                wbt = []
                if g == 0:
                    wbt = [('C.dcy', j), ('B.act', 9)]
                    if j >= 1:
                        wbt.append(('C.yev', j - 1, 15))
                if g >= 2:
                    wbt.append(('C.bt', j, g - 2))
                qt.op(lambda e, g_=g, j_=j: e.transpose(
                    bankbf(4 + g_ % 2, (g_ // 2)*128),
                    AP(bfm, g_*1024 + j_*128, [[8*1024, 128], [1, 128]]),
                    idb[:, :]), sem='tr', inc=1, label=('C.tbt', j, g), waits=wbt)
            for g in range(G):
                wbte = [('C.tbt', j, g)]
                if g == 0 and j >= 1:
                    wbte.append(('C.smm', j - 1, 7))
                qv.op(lambda e, g_=g: e.tensor_copy(
                    btb[:, g_*128:(g_+1)*128], bankbf(4 + g_ % 2, (g_ // 2)*128)), sem='v',
                    label=('C.bt', j, g), waits=wbte)
            for g in range(G):
                b_ = 4 + g // 4; o_ = (g % 4) * 128
                wgm = []
                if g == 0:
                    wgm = [('B.act', 17), ('C.bt', j, 6), ('C.bt', j, 7)]
                    if j >= 1:
                        wgm.append(('C.gmu', j - 1))
                qt.op(lambda e, g_=g, j_=j, b__=b_, o__=o_: e.matmul(
                    bank(b__, 128, o__),
                    AP(bfm, g_*1024 + j_*128, [[8*1024, 128], [1, 128]]),
                    AP(cfm, g_*1024 + j_*128, [[8*1024, 128], [1, 128]])),
                    sem='mm2', inc=1, label=('C.gm', j, g), waits=wgm)
            qv.op(lambda e: e.tensor_mul(
                gmub[:, 0:512], bank(4, 512),
                AP(umb, 0, [[128, 128], [0, 4], [1, 128]])), sem='v',
                waits=[('C.gm', j, 3)] + ([('C.mt', j - 1, 15)] if j >= 1 else []))
            qv.op(lambda e: e.tensor_mul(
                gmub[:, 512:1024], bank(5, 512),
                AP(umb, 0, [[128, 128], [0, 4], [1, 128]])), sem='v',
                label=('C.gmu', j), waits=[('C.gm', j, 7)])
            for g in range(G):
                pr = (g % 2) * 2
                wxd = [('C.X', j), ('C.dcy', j)]
                if g >= 2:
                    wxd.append(('C.smm', j, g - 2))
                qv.op(lambda e, g_=g: e.tensor_mul(
                    AP(xdg, (g_ % 2)*1024, [[2*1024, 128], [1, 1024]]),
                    xj[:, g_*GSL:(g_+1)*GSL],
                    AP(dcyb, g_*16, [[128, 128], [1, 16], [0, 64]])), sem='v',
                    label=('C.xd', j, g), waits=wxd)
                wsm = [('C.xd', j, g), ('C.bt', j, g)]
                if g == 0:
                    wsm.append(('C.acst', j))
                if g == 1:
                    wsm.append(('C.dexq', j))
                if g >= 2:
                    wsm.append(('C.radd', j, g - 2))
                elif j == 0:
                    wsm += [('B.ev', 79), ('B.ev', 80)]
                else:
                    wsm.append(('C.rel', j - 1, 14 + (g % 2)))
                for h2 in range(2):
                    qt.op(lambda e, g_=g, h_=h2, p_=pr: e.matmul(
                        bank(p_ + h_, 512),
                        btb[:, g_*128:(g_+1)*128],
                        AP(xdg, (g_ % 2)*1024 + h_*512, [[2*1024, 128], [1, 512]])),
                        sem='mm2', inc=1, label=('C.smm', j, g) if h2 == 1 else None,
                        waits=wsm if h2 == 0 else ())
                wpc = []
                if j == 0 and g == 0:
                    wpc = ['A.rz']
                if j >= 1:
                    wpc = [('C.ymm', j - 1, 2*g + 1)]
                qv.op(lambda e, g_=g: e.tensor_copy(
                    prb[:, g_*GSL:(g_+1)*GSL], rloc[:, g_*GSL:(g_+1)*GSL]), sem='v',
                    label=('C.prc', j, g), waits=wpc)
                qv.op(lambda e, g_=g, j_=j: e.tensor_mul(
                    rloc[:, g_*GSL:(g_+1)*GSL], rloc[:, g_*GSL:(g_+1)*GSL],
                    AP(dexp, j_*128 + g_*16, [[1024, 128], [1, 16], [0, 64]])), sem='v',
                    label=('C.rmul', j, g), waits=[('C.dexq', j), ('C.prc', j, g)])
                qv.op(lambda e, g_=g, p_=pr: e.tensor_add(
                    rloc[:, g_*GSL:(g_+1)*GSL], rloc[:, g_*GSL:(g_+1)*GSL],
                    AP(ps, p_*512, [[4096, 128], [1, 1024]])), sem='v',
                    label=('C.radd', j, g), waits=[('C.smm', j, g), ('C.rmul', j, g)])
            for h in range(17):
                if h < 16:
                    pr = (h % 2) * 2
                    g = h // 2
                    wac = [('C.fl', j, g)]
                    if h == 0:
                        wac.append(('C.radd', j, 6))
                    elif h == 1:
                        wac.append(('C.radd', j, 7))
                    else:
                        wac.append(('C.rel', j, h - 2))
                    for h2 in range(2):
                        qt.op(lambda e, h_=h, q_=h2, p_=pr: e.matmul(
                            bank(p_ + q_, 512), onesr[0:1, :],
                            flatb[0:1, (h_ % 2)*1024 + q_*512:(h_ % 2)*1024 + (q_+1)*512]),
                            sem='mm2', inc=1, label=('C.acl', j, h) if h2 == 1 else None,
                            waits=wac if h2 == 0 else ())
                if h >= 1:
                    hh = h - 1
                    prg = (hh % 2) * 2
                    for q in range(8):
                        HH = hh*8 + q
                        qc.op(lambda e, q_=q, p_=prg, H_=HH: e.activation(
                            relub[:, q_*128:(q_+1)*128],
                            AP(ps, p_*512 + q_*128, [[4096, 128], [1, 128]]),
                            AF.Relu, scale=-1.0,
                            bias=acstm[:, H_:H_+1]), sem='sc',
                            label=('C.rel', j, hh) if q == 7 else None,
                            waits=[('C.acl', j, hh), ('C.acst', j)] if q == 0 else ())
                    wex = []
                    if hh >= 2:
                        wex.append(('C.ymm', j, hh - 2))
                    qc.op(lambda e, h_=hh: e.activation(
                        AP(mtb, (h_ % 2)*1024, [[2*1024, 128], [1, 1024]]),
                        relub[:, :], AF.Exp, scale=-1.0), sem='sc',
                        label=('C.exp', j, hh), waits=wex)
                    qv.op(lambda e, h_=hh: e.tensor_mul(
                        AP(mtb, (h_ % 2)*1024, [[2*1024, 128], [1, 1024]]),
                        AP(mtb, (h_ % 2)*1024, [[2*1024, 128], [1, 1024]]),
                        AP(gmub, (h_ // 2)*128, [[1024, 128], [0, 8], [1, 128]])), sem='v',
                        label=('C.mt', j, hh), waits=[('C.exp', j, hh), ('C.gmu', j)])
                    ydb = 4 + hh % 2
                    for q in range(8):
                        HH = hh*8 + q
                        wyd = []
                        if q == 0:
                            wyd = [('C.mt', j, hh), ('C.X', j)]
                            if hh >= 2:
                                wyd.append(('C.yev', j, hh - 2))
                        qt.op(lambda e, h_=hh, q_=q, H_=HH, b_=ydb: e.matmul(
                            AP(ps, b_*512 + q_*64, [[4096, 128], [1, 64]]),
                            AP(mtb, (h_ % 2)*1024 + q_*128, [[2*1024, 128], [1, 128]]),
                            xj[:, H_*64:(H_+1)*64]),
                            sem=None, waits=wyd)
                    yob = 6 + hh % 2
                    wyo = [('C.prc', j, hh // 2)]
                    if hh >= 2:
                        wyo.append(('C.yev', j, hh - 2))
                    elif j >= 1:
                        wyo.append(('C.yev', j - 1, 15))
                    qt.op(lambda e, h_=hh, b_=yob, g_=hh // 2, s_=(hh % 2), j_=j: e.matmul(
                        bank(b_, 512),
                        AP(cfm, g_*1024 + j_*128, [[8*1024, 128], [1, 128]]),
                        prb[:, g_*GSL + s_*512:g_*GSL + (s_+1)*512]),
                        sem='mm2', inc=1, label=('C.ymm', j, hh), waits=wyo)
                    wye = [('C.ymm', j, hh), ('C.sdo', j), ('C.dhs', j)]
                    if hh == 0 and j >= 1:
                        wye.append(('C.yst', j - 1))
                    qv.op(lambda e, h_=hh, b_=yob: e.tensor_mul(
                        yevt[:, :], bank(b_, 512),
                        AP(sdob, h_*8, [[128, 128], [1, 8], [0, 64]])), sem='v',
                        waits=wye)
                    qv.op(lambda e, h_=hh, b_=ydb: e.tensor_add(
                        yevt[:, :], yevt[:, :], bank(b_, 512)), sem='v')
                    qv.op(lambda e, h_=hh: e.tensor_add(
                        ylb[:, h_*512:(h_+1)*512], yevt[:, :],
                        dhs[:, h_*512:(h_+1)*512]), sem='v',
                        label=('C.yev', j, hh))

    def phase_Cdbg():
        if 'd_yloc' in DBG:
            qs.op(lambda e: e.dma_start(DBG['d_yloc'][:, :], yloc_scr[:, :]), sem='dbg',
                  inc=16, label='dbg.d_yloc', waits=[('C.yst', 6), ('C.yst', 7)])
        if 'd_T' in DBG:
            qs.op(lambda e: e.dma_start(DBG['d_T'][:, :], rloc[:, :]), sem='dbg',
                  inc=16, label='dbg.d_T', waits=[('C.radd', 7, 6), ('C.radd', 7, 7)])

    oD = PHB
    def _ald(x):
        return (x + 31) & ~31
    tstg = sal('tstg', [128, 2*8192], F32, oD); oD = _ald(oD + 2*8192*4)
    tmpacc = sal('tmpacc', [128, 8192], F32, oD); oD = _ald(oD + 8192*4)
    asall = sal('asall', [8, 128], F32, oD); oD = _ald(oD + 128*4)
    wsb = sal('wsb', [8, 128], F32, oD); oD = _ald(oD + 128*4)
    wflat = sal('wflat', [1, 1024], F32, oD); oD = _ald(oD + 1024*4)
    wexpb = sal('wexpb', [128, 7*128], F32, oD); oD = _ald(oD + 7*128*4)
    assert oD <= SBTOP, oD

    def phase_D():
        qs.op(lambda e: e.dma_start(aginR[:, :], rloc[:, :]), sem='st0', inc=16,
              label='D.rs', waits=[('C.radd', 7, 7)])
        qg.op(lambda e: e.dma_start(
            AP(aginA, 0, [[128, 1], [1, 128]]), AP(asum_s, 0, [[4, 128], [1, 1]])),
            sem='gd0', inc=16, label='D.as', waits=['B.asum'])
        qg.op(lambda e: e.collective_compute(
            'AllGather', OP.bypass, replica_groups=[list(range(NCORES))],
            ins=[aginR.ap()], outs=[agoutR.ap()]), sem='cc', inc=1,
            label='D.ccR', waits=['D.rs'])
        qg.op(lambda e: e.collective_compute(
            'AllGather', OP.bypass, replica_groups=[list(range(NCORES))],
            ins=[aginA.ap()], outs=[agoutA.ap()]), sem='cc', inc=1,
            label='D.ccA', waits=['D.as'])
        qg.op(lambda e: e.dma_start(asall[:, :], agoutA[:, :]), sem='gd0', inc=16,
              label='D.asl', waits=['D.ccA', ('C.ymm', 7, 15)])
        qt.op(lambda e: e.matmul(AP(ps, 0, [[4096, 8], [1, 128]]),
                                 msel_s[:, :], asall[:, :]),
              sem='mm2', inc=1, label='D.mm',
              waits=['D.asl', ('C.rel', 7, 14), ('C.rel', 7, 15)])
        qc.op(lambda e: e.activation(wsb[:, :], AP(ps, 0, [[4096, 8], [1, 128]]), AF.Exp),
              sem='sc', label='D.we', waits=['D.mm'])
        qv.op(lambda e: e.tensor_scalar_mul(wsb[:, :], wsb[:, :], mmask_s[:, :]),
              sem='v', label='D.wm', waits=['D.we', ('A.c', 12)])
        qg.op(lambda e: e.dma_start(
            AP(wflat, 0, [[1024, 1], [128, 8], [1, 128]]),
            AP(wsb, 0, [[128, 8], [1, 128]])), sem='gd0', inc=16,
            label='D.wf', waits=['D.wm'])
        for jj in range(7):
            b_ = 1 + jj % 2; o_ = (jj // 2) * 128
            w = ['D.wf'] if jj == 0 else []
            if jj >= 2:
                w.append(('D.wxe', jj - 2))
            qt.op(lambda e, b__=b_, o__=o_, j_=jj: e.matmul(
                bank(b__, 128, o__), onesr[0:1, :], wflat[0:1, j_*128:(j_+1)*128]),
                sem='mm2', inc=1, label=('D.wx', jj), waits=w)
        for jj in range(7):
            b_ = 1 + jj % 2; o_ = (jj // 2) * 128
            qv.op(lambda e, b__=b_, o__=o_, j_=jj: e.tensor_copy(
                wexpb[:, j_*128:(j_+1)*128], bank(b__, 128, o__)), sem='v',
                label=('D.wxe', jj), waits=[('D.wx', jj)])
        for jj in range(7):
            w = ['D.ccR', ('C.ymm', 7, 15), ('C.yev', 7, 15)] if jj == 0 else []
            if jj >= 2:
                w.append(('D.acc', jj - 2))
            qs.op(lambda e, j_=jj: e.dma_start(
                AP(tstg, (j_ % 2)*8192, [[2*8192, 128], [1, 8192]]),
                AP(agoutR, j_*128*INTER, [[INTER, 128], [1, INTER]])),
                sem='ld%d' % (jj % 2), inc=16, label=('D.tl', jj), waits=w)
        qv.op(lambda e: e.memset(rloc[:, :], 0.0), sem='v', label='D.rz', waits=['D.rs'])
        for jj in range(7):
            wv = [('D.tl', jj), ('D.wxe', jj), 'D.rz']
            if jj == 0:
                wv += [('C.yst', 6), ('C.yst', 7)]
            qv.op(lambda e, j_=jj: e.tensor_mul(
                tmpacc[:, :], AP(tstg, (j_ % 2)*8192, [[2*8192, 128], [1, 8192]]),
                AP(wexpb, j_*128, [[7*128, 128], [1, 128], [0, 64]])), sem='v',
                waits=wv)
            qv.op(lambda e: e.tensor_add(rloc[:, :], rloc[:, :], tmpacc[:, :]),
                  sem='v', label=('D.acc', jj))

    def phase_Ddbg():
        if 'd_init' in DBG:
            qs.op(lambda e: e.dma_start(DBG['d_init'][:, :], rloc[:, :]), sem='dbg',
                  inc=16, label='dbg.d_init', waits=[('D.acc', 6)])

    oE1 = PHB
    ylb2 = sal('ylb2', [128, 2*8192], BF16, oE1); oE1 = _ald(oE1 + 2*8192*2)
    ipb = sal('ipb', [128, 8192], BF16, oE1); oE1 = _ald(oE1 + 8192*2)
    acstm2 = sal('acstm2', [128, 128], F32, oE1); oE1 = _ald(oE1 + 128*4)
    sdob2 = sal('sdob2', [128, 128], BF16, oE1); oE1 = _ald(oE1 + 128*2)
    yevt2 = sal('yevt2', [128, 512], BF16, oE1); oE1 = _ald(oE1 + 512*2)
    assert oE1 <= SBTOP, oE1

    def phase_E1():
        for j in range(NCH):
            par = j % 2
            w = [('C.yst', j)] if j < 2 else [('C.yst', j), ('E1.ys', j - 2)]
            qs.op(lambda e, j_=j, p_=par: e.dma_start(
                AP(ylb2, p_*8192, [[2*8192, 128], [1, 8192]]),
                AP(yloc_scr, j_*128*INTER, [[INTER, 128], [1, INTER]])),
                sem='ld%d' % par, inc=16, label=('E1.yl', j),
                waits=w + ([('D.acc', 6)] if j < 2 else []))
            if j >= 1:
                qv.op(lambda e, j_=j: e.tensor_mul(
                    rloc[:, :], rloc[:, :],
                    AP(dexp, (j_-1)*128, [[1024, 128], [1, 128], [0, 64]])), sem='v',
                    label=('E1.im', j), waits=[('D.acc', 6), ('E1.ipc', j - 1)])
            qv.op(lambda e: e.tensor_copy(ipb[:, :], rloc[:, :]), sem='v',
                  label=('E1.ipc', j),
                  waits=[('D.acc', 6)] if j == 0 else [('E1.ymm', j - 1, 15)])
            if j == 0:
                wtr = [('C.yev', 7, 15), ('C.dexq', 7), ('D.wx', 6), ('D.acc', 6)]
            else:
                wtr = [('E1.sdq', j - 1)]
            qt.op(lambda e, j_=j: e.transpose(
                bank(6, 128), acsf[:, j_*128:(j_+1)*128], idf[:, :]),
                sem='tr', inc=1, label=('E1.tacs', j),
                waits=wtr + ([('E1.yev', j - 1, 15)] if j >= 1 else []))
            qv.op(lambda e: e.tensor_copy(acstm2[:, :], bank(6, 128)), sem='v',
                  label=('E1.sdq', j), waits=[('E1.tacs', j)])
            qc.op(lambda e: e.activation(sdob2[:, :], acstm2[:, :], AF.Exp), sem='sc',
                  label=('E1.sdo', j), waits=[('E1.sdq', j)])
            for h in range(16):
                g = h // 2
                b_ = h % 8
                w = [('E1.ipc', j)]
                if h == 0 and j == 0:
                    w += ['D.we', ('D.wxe', 6), ('C.yev', 7, 15), ('C.rel', 7, 15)]
                if h >= 8:
                    w.append(('E1.yev', j, h - 8))
                elif j >= 1:
                    w.append(('E1.yev', j - 1, h + 8))
                if h == 6:
                    w.append(('E1.sdq', j))
                qt.op(lambda e, h_=h, g_=g, b__=b_, j_=j: e.matmul(
                    bank(b__, 512),
                    AP(cfm, g_*1024 + j_*128, [[8*1024, 128], [1, 128]]),
                    ipb[:, g_*1024 + (h_ % 2)*512:g_*1024 + (h_ % 2)*512 + 512]),
                    sem='mm2', inc=1, label=('E1.ymm', j, h), waits=w)
                qv.op(lambda e, h_=h, b__=b_: e.tensor_mul(
                    yevt2[:, :], bank(b__, 512),
                    AP(sdob2, h_*8, [[128, 128], [1, 8], [0, 64]])), sem='v',
                    waits=[('E1.ymm', j, h), ('E1.sdo', j), ('E1.yl', j)])
                qv.op(lambda e, h_=h, p_=par: e.tensor_add(
                    AP(ylb2, p_*8192 + h_*512, [[2*8192, 128], [1, 512]]),
                    AP(ylb2, p_*8192 + h_*512, [[2*8192, 128], [1, 512]]),
                    yevt2[:, :]), sem='v', label=('E1.yev', j, h))
            qs.op(lambda e, j_=j, p_=par: e.dma_start(
                AP(yfull_scr, j_*128*INTER, [[INTER, 128], [1, INTER]]),
                AP(ylb2, p_*8192, [[2*8192, 128], [1, 8192]])),
                sem='st%d' % par, inc=16, label=('E1.ys', j), waits=[('E1.yev', j, 15)])

    def phase_E1dbg():
        if 'd_yfull' in DBG:
            qs.op(lambda e: e.dma_start(DBG['d_yfull'][:, :], yfull_scr[:, :]), sem='dbg',
                  inc=16, label='dbg.d_yfull', waits=[('E1.ys', 6), ('E1.ys', 7)])

    EB = LBASE
    gslab = sal('gslab', [128, 2*16384], BF16, EB)
    gstg = sal('gstg', [128, 2*4096], BF16, 90240)
    xTb2 = sal('xTb2', [128, NKT*TW], BF16, 106624)

    def phase_E2():
        qg.op(lambda e: e.dma_start(
            AP(xTb2, 0, [[NKT*TW, 128], [TW, NKT], [1, TW]]),
            AP(D['xT_ch'], 0, [[TW, 128], [128*TW, NKT], [1, TW]])),
            sem='xld', inc=16, label='E2.x',
            waits=[('E1.ymm', 7, 15), ('E1.tacs', 7), ('E1.ys', 6), ('E1.ys', 7)])
        for ot in range(16):
            par = ot % 2
            w = [('E2.mm', ot - 2)] if ot >= 2 else []
            qg.op(lambda e, o_=ot, p_=par: e.dma_start(
                AP(gslab, p_*16384, [[2*16384, 128], [512, NKT], [1, 512]]),
                AP(D['wgt'], o_*NKT*128*512, [[512, 128], [128*512, NKT], [1, 512]])),
                sem='wld%d' % par, inc=16, label=('E2.w', ot), waits=w)
        qt.wait('E2.x')
        for ot in range(16):
            par = ot % 2
            for kt in range(NKT):
                for tc in range(8):
                    w = []
                    if kt == 0 and tc == 0:
                        w = [('E2.w', ot)]
                        if ot == 0:
                            w += [('E1.yev', 7, 15), ('E1.sdq', 7)]
                        else:
                            w.append(('E2.sil', ot - 1, 7))
                    qt.op(lambda e, k_=kt, t_=tc, p_=par: e.matmul(
                        bank(t_, 512),
                        AP(xTb2, k_*TW + 3 + t_*128, [[NKT*TW, 128], [1, 128]]),
                        AP(gslab, p_*16384 + k_*512, [[2*16384, 128], [1, 512]]),
                        start=(k_ == 0), stop=(k_ == NKT - 1)),
                        sem='mm' if (kt == NKT - 1 and tc == 7) else None, inc=1,
                        label=('E2.mm', ot) if (kt == NKT - 1 and tc == 7) else None,
                        waits=w)
            for tc in range(8):
                w = [('E2.mm', ot)]
                if ot >= 2 and tc == 0:
                    w.append(('E2.gst', ot - 2))
                qc.op(lambda e, t_=tc, p_=par: e.activation(
                    AP(gstg, p_*4096 + t_*512, [[2*4096, 128], [1, 512]]),
                    bank(t_, 512), AF.Silu), sem='sc',
                    label=('E2.sil', ot, tc), waits=w)
            qs.op(lambda e, o_=ot, p_=par: e.dma_start(
                AP(gate_scr, o_*512, [[16*512, 128], [128*16*512, 8], [1, 512]]),
                AP(gstg, p_*4096, [[2*4096, 128], [512, 8], [1, 512]])),
                sem='st%d' % par, inc=16, label=('E2.gst', ot),
                waits=[('E2.sil', ot, 7)])

    normt = sal('normt', [128, NCH*NHS*128], BF16, EB)
    gb3 = sal('gb3', [128, 8192], BF16, 155776)
    yb3 = sal('yb3', [128, 8192], BF16, 172160)
    xgb = sal('xgb', [128, 8192], BF16, 188544)
    sqd = sal('sqd', [128, 1024], BF16, 204928)
    ssq = sal('ssq', [128, 8], F32, 207008)
    rsq = sal('rsq', [128, 8], F32, 207040)
    rtmp = sal('rtmp', [128, 8], F32, 207072)

    def phase_E3():
        for j in range(NCH):
            par = j % 2
            wg = [('E2.gst', 14), ('E2.gst', 15)] if j == 0 else [('E3.xg', j - 1)]
            qs.op(lambda e, j_=j: e.dma_start(
                gb3[:, :],
                AP(gate_scr, j_*128*16*512, [[16*512, 128], [1, 8192]])),
                sem='ld%d' % par, inc=16, label=('E3.gl', j), waits=wg)
            wy = [('E1.ys', j)] + ([('E3.xg', j - 1)] if j >= 1 else [])
            qs.op(lambda e, j_=j: e.dma_start(
                yb3[:, :],
                AP(yfull_scr, j_*128*INTER, [[INTER, 128], [1, INTER]])),
                sem='st%d' % par, inc=16,
                label=('E3.ylq', j), waits=wy)
            wxg = [('E3.gl', j), ('E3.ylq', j)]
            if j >= 1:
                wxg.append(('E3.tn', j - 1, 63))
            qv.op(lambda e: e.tensor_mul(
                xgb[:, :], yb3[:, :], gb3[:, :]), sem='v',
                label=('E3.xg', j), waits=wxg)
            for k in range(8):
                qc.op(lambda e, k_=k: e.activation(
                    sqd[:, :], AP(xgb, k_*128, [[8192, 128], [1024, 8], [1, 128]]),
                    AF.Square, accum_out=ssq[:, k_:k_+1]), sem='sc',
                    label=('E3.sq', j, k), waits=[('E3.xg', j)] if k == 0 else ())
            qv.op(lambda e: e.tensor_scalar(
                rsq[:, :], ssq[:, :], 1.0/1024.0, EPS, OP.mult, OP.add), sem='v',
                label=('E3.ms', j), waits=[('E3.sq', j, 7)])
            qc.op(lambda e: e.activation(rtmp[:, :], rsq[:, :], AF.Ln), sem='sc',
                  label=('E3.ln', j), waits=[('E3.ms', j)])
            qc.op(lambda e: e.activation(rtmp[:, :], rtmp[:, :], AF.Exp, scale=-0.5),
                  sem='sc', label=('E3.rs', j), waits=[('E3.ln', j)])
            for k in range(8):
                qv.op(lambda e, k_=k: e.tensor_scalar_mul(
                    AP(xgb, k_*128, [[8192, 128], [1024, 8], [1, 128]]),
                    AP(xgb, k_*128, [[8192, 128], [1024, 8], [1, 128]]),
                    rtmp[:, k_:k_+1]), sem='v',
                    label=('E3.xr', j, k), waits=[('E3.rs', j)] if k == 0 else ())
            for kt in range(NHS):
                w = [('E3.xr', j, 7)] if kt == 0 else []
                if kt == 0 and j == 0:
                    w.append(('E2.sil', 15, 7))
                bslot = 6 + (kt % 2)
                if kt >= 2:
                    w.append(('E3.ne', j, kt - 2))
                qt.op(lambda e, k_=kt, b_=bslot: e.transpose(
                    bankbf(b_, 0), xgb[:, k_*128:(k_+1)*128], idb[:, :]),
                    sem='tr', inc=1, label=('E3.tn', j, kt), waits=w)
                qv.op(lambda e, k_=kt, j_=j, b_=bslot: e.tensor_scalar_mul(
                    normt[:, (j_*NHS + k_)*128:(j_*NHS + k_ + 1)*128],
                    bankbf(b_, 0), nw_s[:, k_:k_+1]), sem='v',
                    label=('E3.ne', j, kt), waits=[('E3.tn', j, kt)])

    def phase_E3dbg():
        if 'd_normt' in DBG:
            qs.op(lambda e: e.dma_start(DBG['d_normt'][:, :], normt[:, :]), sem='dbg',
                  inc=16, label='dbg.d_normt', waits=[('E3.ne', 7, 62), ('E3.ne', 7, 63)])

    fslab = sal('fslab', [128, 2*8192], BF16, 155776)
    fstg = sal('fstg', [128, 8*512], BF16, 188544)

    def phase_F():
        for ot in range(8):
            for ks in range(4):
                sl = ot*4 + ks
                par = sl % 2
                w = [('F.mm', ot - 1, 3)] if (ks <= 1 and ot >= 1) else                     ([('F.mm', ot, ks - 2)] if ks >= 2 else [])
                if ot == 0 and ks == 0:
                    w = [('E3.xg', 7)]
                qg.op(lambda e, o_=ot, k_=ks, p_=par: e.dma_start(
                    AP(fslab, p_*8192, [[2*8192, 128], [512, 16], [1, 512]]),
                    AP(D['wot'], (o_*4 + k_)*16*128*512, [[512, 128], [128*512, 16], [1, 512]])),
                    sem='wld%d' % par, inc=16, label=('F.w', ot, ks), waits=w)
            for ks in range(4):
                par = (ot*4 + ks) % 2
                for kt in range(16):
                    for tc in range(8):
                        w = []
                        if kt == 0 and tc == 0:
                            w = [('F.w', ot, ks)]
                            if ks == 0:
                                if ot == 0:
                                    w += [('E3.ne', 7, 62), ('E3.ne', 7, 63),
                                          ('E2.sil', 15, 7)]
                                else:
                                    w.append(('F.ev', ot - 1, 7))
                        ktg = ks*16 + kt
                        qt.op(lambda e, k_=ktg, t_=tc, p_=par, s0=(ks == 0 and kt == 0),
                              s1=(ks == 3 and kt == 15): e.matmul(
                            bank(t_, 512),
                            AP(normt, t_*NHS*128 + k_*128, [[NCH*NHS*128, 128], [1, 128]]),
                            AP(fslab, p_*8192 + (k_ % 16)*512, [[2*8192, 128], [1, 512]]),
                            start=s0, stop=s1),
                            sem='mm2', inc=1,
                            label=('F.mm', ot, ks) if (kt == 15 and tc == 7) else None,
                            waits=w)
            for tc in range(8):
                qv.op(lambda e, t_=tc: e.tensor_copy(
                    fstg[:, t_*512:(t_+1)*512], bank(t_, 512)), sem='v',
                    label=('F.ev', ot, tc),
                    waits=[('F.mm', ot, 3)] +
                          (([('F.st', ot - 1)] if ot >= 1 else []) +
                           ([('F.st', ot - 2)] if ot >= 2 else []) if tc == 0 else []))
            qs.op(lambda e, o_=ot: e.dma_start(
                AP(x2_scr, o_*128*512, [[512, 128], [8*128*512, 8], [1, 512]]),
                AP(fstg, 0, [[8*512, 128], [512, 8], [1, 512]])),
                sem='st%d' % (ot % 2), inc=16, label=('F.st', ot), waits=[('F.ev', ot, 7)])

    x2nt = sal('x2nt', [128, NCH*NKT*128], BF16, EB)
    x2b = sal('x2b', [128, 2*4096], BF16, 119040)
    xrb = sal('xrb', [128, 2*4096], F32, 135424)
    x2f = sal('x2f', [128, 4096], F32, 168192)
    x2nf = sal('x2nf', [128, 4096], F32, 184576)
    x2nb = sal('x2nb', [128, 4096], BF16, 200960)
    mnv = sal('mnv', [128, 4], F32, 209152)
    vrv = sal('vrv', [128, 4], F32, 209184)
    rst = sal('rst', [128, 4], F32, 209216)
    sqg = sal('sqg', [128, 4096], BF16, 209248)

    def phase_G():
        for j in range(NCH):
            par = j % 2
            wl = [('F.st', 6), ('F.st', 7)] if j == 0 else ([('G.tx', j - 2, 31)] if j >= 2 else [])
            qs.op(lambda e, j_=j, p_=par: e.dma_start(
                AP(x2b, p_*4096, [[2*4096, 128], [1, 4096]]),
                AP(x2_scr, j_*8*128*512, [[512, 128], [128*512, 8], [1, 512]])),
                sem='ld%d' % par, inc=16, label=('G.al', j), waits=wl)
            qs.op(lambda e, j_=j, p_=par: e.dma_start(
                AP(xrb, p_*4096, [[2*4096, 128], [1, 4096]]),
                AP(D['xrows'], j_*128*HID, [[HID, 128], [1, 4096]])),
                sem='st%d' % par, inc=16, label=('G.xl', j),
                waits=[('G.x2', j - 2)] if j >= 2 else [('F.st', 6), ('F.st', 7)])
            wx2 = [('G.al', j), ('G.xl', j)]
            if j >= 1:
                wx2.append(('G.x2s', j - 1))
            qv.op(lambda e, p_=par: e.tensor_add(
                x2f[:, :], AP(xrb, p_*4096, [[2*4096, 128], [1, 4096]]),
                AP(x2b, p_*4096, [[2*4096, 128], [1, 4096]])), sem='v',
                label=('G.x2', j), waits=wx2)
            qv.op(lambda e: e.tensor_reduce(
                mnv[:, 0:1], x2f[:, :], mybir.AxisListType.X, OP.add), sem='v',
                label=('G.sum', j))
            qc.op(lambda e: e.activation(
                sqg[:, :], x2f[:, :], AF.Square, accum_out=vrv[:, 0:1]), sem='sc',
                label=('G.sq', j), waits=[('G.x2', j)])
            qv.op(lambda e: e.tensor_scalar_mul(mnv[:, 0:1], mnv[:, 0:1], 1.0/4096.0),
                  sem='v', label=('G.m', j))
            qv.op(lambda e: e.tensor_mul(rst[:, 0:1], mnv[:, 0:1], mnv[:, 0:1]), sem='v')
            qv.op(lambda e: e.tensor_scalar(
                vrv[:, 0:1], vrv[:, 0:1], 1.0/4096.0, EPS, OP.mult, OP.add), sem='v',
                waits=[('G.sq', j)])
            qv.op(lambda e: e.tensor_sub(vrv[:, 0:1], vrv[:, 0:1], rst[:, 0:1]), sem='v',
                  label=('G.v', j))
            qc.op(lambda e: e.activation(rst[:, 0:1], vrv[:, 0:1], AF.Ln), sem='sc',
                  label=('G.ln', j), waits=[('G.v', j)])
            qc.op(lambda e: e.activation(rst[:, 0:1], rst[:, 0:1], AF.Exp, scale=-0.5),
                  sem='sc', label=('G.rs', j))
            qv.op(lambda e: e.tensor_scalar(
                x2nf[:, :], x2f[:, :], mnv[:, 0:1], rst[:, 0:1],
                OP.subtract, OP.mult), sem='v', label=('G.n', j),
                waits=[('G.rs', j), ('G.m', j)] + ([('G.x2s', j - 1)] if j >= 1 else []))
            qs.op(lambda e, j_=j: e.dma_start(
                AP(x2n_scr, j_*128*HID, [[HID, 128], [1, 4096]]), x2nf[:, :]),
                sem='st2', inc=16,
                label=('G.x2s', j), waits=[('G.n', j)])
            qv.op(lambda e: e.tensor_copy(x2nb[:, :], x2nf[:, :]), sem='v',
                  label=('G.nb', j), waits=[('G.tx', j - 1, 31)] if j >= 1 else [])
            for kt in range(NKT):
                w = [('G.nb', j)] if kt == 0 else []
                if kt == 0 and j == 0:
                    w.append(('F.ev', 7, 7))
                bslot = 6 + (kt % 2)
                if kt >= 2:
                    w.append(('G.te', j, kt - 2))
                qt.op(lambda e, k_=kt, b_=bslot: e.transpose(
                    bankbf(b_, 0),
                    AP(x2nb, (k_ % 16)*256 + k_//16, [[4096, 128], [2, 128]]),
                    idb[:, :]),
                    sem='tr', inc=1, label=('G.tx', j, kt), waits=w)
                qv.op(lambda e, k_=kt, j_=j, b_=bslot: e.tensor_copy(
                    x2nt[:, (j_*NKT + k_)*128:(j_*NKT + k_ + 1)*128], bankbf(b_, 0)),
                    sem='v', label=('G.te', j, kt), waits=[('G.tx', j, kt)])

    def phase_Gdbg():
        if 'd_x2n' in DBG:
            qs.op(lambda e: e.dma_start(DBG['d_x2n'][:, :], x2n_scr[:, :]), sem='dbg',
                  inc=16, label='dbg.d_x2n', waits=[('G.x2s', 7)])

    pkb = sal('pkb', [128, 2*8192], I32, 90240)
    scb = sal('scb', [128, 16384], BF16, 155776)
    wqi = sal('wqi', [128, 2*512], I32, 196736)
    wqb = sal('wqb', [128, 2*512], BF16, 200832)
    hstg = sal('hstg', [128, 8*512], BF16, 188544)

    def phase_H():
        for nt in range(8):
            par = nt % 2
            w = []
            if nt >= 2:
                w = [('H.mm', nt - 2, 31)]
            elif nt == 0:
                w = [('G.n', 7), ('G.x2s', 7), ('G.nb', 7), ('G.tx', 7, 31)]
            qg.op(lambda e, n_=nt, p_=par: e.dma_start(
                AP(pkb, p_*8192, [[2*8192, 128], [512, 16], [1, 512]]),
                AP(D['pkd'], n_*16*128*512, [[512, 128], [128*512, 16], [1, 512]])),
                sem='wld%d' % par, inc=16, label=('H.pk', nt), waits=w)
            wsc = [('H.dq', nt - 1, 31)] if nt >= 1 else \
                  [('G.tx', 7, 31), ('G.n', 7), ('G.x2s', 7)]
            qg.op(lambda e, n_=nt: e.dma_start(
                AP(scb, 0, [[16384, 128], [512, 32], [1, 512]]),
                AP(D['scr'], n_*32*128*512, [[512, 128], [128*512, 32], [1, 512]])),
                sem='gd1', inc=16, label=('H.sc', nt), waits=wsc)
            for kt in range(32):
                wdq = []
                if kt == 0:
                    wdq = [('H.pk', nt), ('H.sc', nt)]
                if kt >= 2:
                    wdq.append(('H.mm', nt, kt - 2))
                elif nt >= 1:
                    wdq.append(('H.mm', nt - 1, 30 + kt))
                qv.op(lambda e, k_=kt, p_=par: e.tensor_scalar(
                    wqi[:, (k_ % 2)*512:(k_ % 2)*512 + 512],
                    AP(pkb, p_*8192 + (k_ % 16)*512, [[2*8192, 128], [1, 512]]),
                    (k_ // 16)*4, 15,
                    OP.logical_shift_right, OP.bitwise_and), sem='v',
                    label=('H.d1', nt, kt), waits=wdq)
                qv.op(lambda e, k_=kt: e.scalar_tensor_tensor(
                    wqb[:, (k_ % 2)*512:(k_ % 2)*512 + 512],
                    wqi[:, (k_ % 2)*512:(k_ % 2)*512 + 512],
                    8.0,
                    AP(scb, k_*512, [[16384, 128], [1, 512]]),
                    OP.subtract, OP.mult), sem='v',
                    label=('H.dq', nt, kt))
                for tc in range(8):
                    w2 = []
                    if tc == 0:
                        w2 = [('H.dq', nt, kt)]
                        if kt == 0:
                            if nt == 0:
                                w2 += [('G.te', 7, 30), ('G.te', 7, 31)]
                            else:
                                w2.append(('H.ev', nt - 1, 7))
                    qt.op(lambda e, k_=kt, t_=tc: e.matmul(
                        bank(t_, 512),
                        AP(x2nt, t_*NKT*128 + k_*128, [[NCH*NKT*128, 128], [1, 128]]),
                        wqb[:, (k_ % 2)*512:(k_ % 2)*512 + 512],
                        start=(k_ == 0), stop=(k_ == 31)),
                        sem='mm2', inc=1,
                        label=('H.mm', nt, kt) if tc == 7 else None,
                        waits=w2)
            for tc in range(8):
                qv.op(lambda e, t_=tc: e.tensor_copy(
                    hstg[:, t_*512:(t_+1)*512], bank(t_, 512)), sem='v',
                    label=('H.ev', nt, tc),
                    waits=[('H.mm', nt, 31)] +
                          (([('H.st', nt - 1)] if nt >= 1 else []) +
                           ([('H.st', nt - 2)] if nt >= 2 else []) if tc == 0 else []))
            qs.op(lambda e, n_=nt: e.dma_start(
                AP(x3_scr, n_*128*512, [[512, 128], [8*128*512, 8], [1, 512]]),
                AP(hstg, 0, [[8*512, 128], [512, 8], [1, 512]])),
                sem='st%d' % (nt % 2), inc=16, label=('H.st', nt), waits=[('H.ev', nt, 7)])

    def phase_I():
        for j in range(NCH):
            par = j % 2
            wl = [('H.st', 6), ('H.st', 7)] if j == 0 else                  ([('I.n', j - 2)] if j >= 2 else [])
            qs.op(lambda e, j_=j, p_=par: e.dma_start(
                AP(x2b, p_*4096, [[2*4096, 128], [1, 4096]]),
                AP(x3_scr, j_*8*128*512, [[512, 128], [128*512, 8], [1, 512]])),
                sem='ld%d' % par, inc=16, label=('I.al', j), waits=wl)
            qs.op(lambda e, j_=j, p_=par: e.dma_start(
                AP(xrb, p_*4096, [[2*4096, 128], [1, 4096]]),
                AP(x2n_scr, j_*128*HID, [[HID, 128], [1, 4096]])),
                sem='st%d' % par, inc=16, label=('I.xl', j),
                waits=[('G.x2s', 7)] + ([('I.n', j - 2)] if j >= 2 else
                                        [('H.st', 6), ('H.st', 7)]))
            wx2 = [('I.al', j), ('I.xl', j)]
            if j >= 1:
                wx2.append(('I.os', j - 1))
            qv.op(lambda e, p_=par: e.tensor_add(
                x2f[:, :], AP(xrb, p_*4096, [[2*4096, 128], [1, 4096]]),
                AP(x2b, p_*4096, [[2*4096, 128], [1, 4096]])), sem='v',
                label=('I.x3', j), waits=wx2)
            qv.op(lambda e: e.tensor_reduce(
                mnv[:, 0:1], x2f[:, :], mybir.AxisListType.X, OP.add), sem='v')
            qc.op(lambda e: e.activation(
                sqg[:, :], x2f[:, :], AF.Square, accum_out=vrv[:, 0:1]), sem='sc',
                label=('I.sq', j), waits=[('I.x3', j)])
            qv.op(lambda e: e.tensor_scalar_mul(mnv[:, 0:1], mnv[:, 0:1], 1.0/4096.0),
                  sem='v')
            qv.op(lambda e: e.tensor_mul(rst[:, 0:1], mnv[:, 0:1], mnv[:, 0:1]), sem='v')
            qv.op(lambda e: e.tensor_scalar(
                vrv[:, 0:1], vrv[:, 0:1], 1.0/4096.0, EPS, OP.mult, OP.add), sem='v',
                waits=[('I.sq', j)])
            qv.op(lambda e: e.tensor_sub(vrv[:, 0:1], vrv[:, 0:1], rst[:, 0:1]), sem='v',
                  label=('I.v', j))
            qc.op(lambda e: e.activation(rst[:, 0:1], vrv[:, 0:1], AF.Ln), sem='sc',
                  label=('I.ln', j), waits=[('I.v', j)])
            qc.op(lambda e: e.activation(rst[:, 0:1], rst[:, 0:1], AF.Exp, scale=-0.5),
                  sem='sc', label=('I.rs', j))
            qv.op(lambda e: e.tensor_scalar(
                x2nf[:, :], x2f[:, :], mnv[:, 0:1], rst[:, 0:1],
                OP.subtract, OP.mult), sem='v', label=('I.n', j),
                waits=[('I.rs', j)] + ([('I.os', j - 1)] if j >= 1 else []))
            qs.op(lambda e, j_=j: e.dma_start(
                AP(out, j_*128*HID, [[HID, 128], [1, 4096]]), x2nf[:, :]),
                sem='gd0' if False else 'dbg', inc=16,
                label=('I.os', j), waits=[('I.n', j)])

    def phase_end():
        fins = []
        for s in ['st0', 'st1', 'gd0', 'gd1', 'cst', 'dbg', 'ld0', 'ld1', 'xld', 'st2', 'gd2']:
            fins.append(('FIN', s))
        qs.op(lambda e: e.nop(), sem='cc', inc=1, label='ENDNOP', waits=fins)

    phase_list = [phase_A, phase_B, phase_Bdbg, phase_C, phase_Cdbg, phase_D, phase_Ddbg, phase_E1, phase_E1dbg, phase_E2, phase_E3, phase_E3dbg, phase_F, phase_G, phase_Gdbg, phase_H, phase_I][:nphase + 1]

    ACTIVE[0] = None
    K.dry = True
    for q in ALLQ.values():
        q.counts = {}
    for f in phase_list:
        f()
    for s in ['st0', 'st1', 'gd0', 'gd1', 'cst', 'dbg', 'ld0', 'ld1', 'xld', 'st2', 'gd2']:
        v = max([q.counts.get(s, 0) for q in ALLQ.values()] + [0])
        K.labels[('FIN', s)] = (s, v)
    phase_end()

    K.dry = False
    with nc.Block() as blk:
        def mkbody(name, q):
            def body(e):
                for qq in ALLQ.values():
                    qq.counts = {}
                q.eng = e
                ACTIVE[0] = name
                for f in phase_list:
                    f()
                phase_end()
            return body
        blk.sync(mkbody('sync', qs))
        blk.gpsimd(mkbody('gpsimd', qg))
        blk.tensor(mkbody('tensor', qt))
        blk.vector(mkbody('vector', qv))
        blk.scalar(mkbody('scalar', qc))
    return nc


_NC_CACHE = {}


def kernel(**inputs):
    from concourse.bass_utils import run_bass_kernel_spmd
    maps = prep_inputs(inputs)
    if 'nc' not in _NC_CACHE:
        nc = build(dbg=(), nphase=99)
        nc.finalize()
        _NC_CACHE['nc'] = nc
    res = run_bass_kernel_spmd(_NC_CACHE['nc'], maps, core_ids=list(range(NCORES)))
    return np.concatenate([r['out'] for r in res.results], 0)[None]
